# version 02
# baseline (speedup 1.0000x reference)
import contextlib
import numpy as np
import ml_dtypes
import concourse.bass as bass
import concourse.mybir as mybir
from concourse.bass_utils import run_bass_kernel_spmd

F32 = mybir.dt.float32
BF16 = mybir.dt.bfloat16
I32 = mybir.dt.int32
AF = mybir.ActivationFunctionType
ALU = mybir.AluOpType
AX = mybir.AxisListType

ENGS = ("sync", "scalar", "vector", "tensor", "gpsimd")


def real_cfg():
    return dict(T=2048, D=2048, I=8192, G=4, HD=64)


def mini_cfg():
    return dict(T=256, D=256, I=512, G=4, HD=64)


def derive(cfg):
    c = dict(cfg)
    c["TILE"] = c["T"] // 2
    c["NSUB"] = c["TILE"] // 128
    c["KC"] = c["D"] // 128
    c["ICT"] = c["I"] // 128
    c["ICH"] = c["ICT"] // 2
    c["D2"] = c["D"] // 2
    c["KVW"] = 2 * c["G"] * c["HD"]
    c["GH"] = c["D"] // c["HD"]
    c["HPG"] = c["GH"] // c["G"]
    return c



def host_prep(cfg, x, Wq, bq, Wkv, bkv, gw, gs, uw, us, dw, ds, n_cores):
    c = derive(cfg)
    T, D, KC, ICT, ICH, D2 = (c[k] for k in ("T", "D", "KC", "ICT", "ICH", "D2"))
    x = np.ascontiguousarray(np.asarray(x, dtype=np.float32).reshape(-1, D))
    assert x.shape[0] == T * n_cores

    def blk_gu(W):
        W4 = np.asarray(W).reshape(ICT, 128, KC, 128)
        return np.ascontiguousarray(W4.transpose(0, 3, 2, 1).reshape(
            ICT, 128, KC * 128))

    def blk_d(W):
        WT = np.ascontiguousarray(np.asarray(W).T)
        W4 = WT.reshape(2, ICH * 128, 2, D2)
        return np.ascontiguousarray(W4.transpose(0, 2, 1, 3))

    shared = dict(
        gwt=blk_gu(gw), uwt=blk_gu(uw), dwt=blk_d(dw),
        gs=np.ascontiguousarray(np.asarray(gs, np.float32).reshape(ICT, 128).T),
        us=np.ascontiguousarray(np.asarray(us, np.float32).reshape(ICT, 128).T),
        ds_r=np.ascontiguousarray(np.asarray(ds, np.float32).reshape(1, D)),
        bq_r=np.ascontiguousarray(np.asarray(bq, np.float32).reshape(1, D)),
        bkv_r=np.ascontiguousarray(np.asarray(bkv, np.float32).reshape(1, c["KVW"])),
        wq=np.ascontiguousarray(np.asarray(Wq, np.float32)),
        wkv=np.ascontiguousarray(np.asarray(Wkv, np.float32)),
        idn=np.eye(128, dtype=np.float32))
    in_maps = []
    for cc in range(n_cores):
        m = dict(shared)
        m["x"] = np.ascontiguousarray(x[cc * T:(cc + 1) * T])
        in_maps.append(m)
    return in_maps



class Build:
    TOKENED = ("vector", "scalar")

    def __init__(self, nc):
        self.nc = nc
        self.ops = {e: [] for e in ENGS}
        self.cnt = {}
        self.sem = {}
        self.aux = {}
        self.last_done = {}

    def sem_named(self, name):
        if name not in self.sem:
            self.sem[name] = self.stack.enter_context(self.nc.semaphore(name))
            self.cnt[name] = 0
        return self.sem[name]

    def n(self, name):
        return self.cnt.get(name, 0)

    def emit(self, eng, fn):
        self.ops[eng].append(fn)

    def wait(self, eng, name, val):
        if val is None or val <= 0:
            return
        s = self.sem_named(name)
        self.emit(eng, lambda E, s=s, v=val: E.wait_ge(s, v))

    def op(self, eng, fn, incs=(), dma=False):
        incs = list(incs)
        if eng in self.TOKENED and not dma:
            prev = self.last_done.get(eng)
            if prev is not None:
                self.wait(eng, prev[0], prev[1])
            if not incs:
                incs.append((eng[0] + "tok", 1))
        incs2 = []
        for name, k in incs:
            s = self.sem_named(name)
            self.cnt[name] = self.cnt.get(name, 0) + k
            incs2.append((s, k))
        if eng in self.TOKENED and not dma and incs:
            self.last_done[eng] = (incs[0][0], self.cnt[incs[0][0]])

        def run(E, fn=fn, incs2=tuple(incs2)):
            ins = fn(E)
            for s, k in incs2:
                ins.then_inc(s, k)
        self.emit(eng, run)

    def barrier(self):
        s = self.sem_named("bar")
        self.cnt["bar"] += len(ENGS)
        tgt = self.cnt["bar"]
        for e in ENGS:
            self.emit(e, lambda E, s=s: E.sem_inc(s, 1))
            self.emit(e, lambda E, s=s, v=tgt: E.wait_ge(s, v))

    def psum_fence(self, eng):
        for s in ("vfl", "vqb", "vkv", "vbc", "vte", "ssi", "vhh"):
            self.wait(eng, s, self.n(s))


def build_program(cfg):
    c = derive(cfg)
    T, D, I, G, HD = (c[k] for k in ("T", "D", "I", "G", "HD"))
    TILE, NSUB, KC, ICT, ICH, D2, KVW, GH = (
        c[k] for k in ("TILE", "NSUB", "KC", "ICT", "ICH", "D2", "KVW", "GH"))
    HPG = c["HPG"]
    EPS = 1e-5
    NMMT = (TILE + 511) // 512
    ND2 = (D2 + 511) // 512
    NQ = (D + 511) // 512
    NKV = (KVW + 511) // 512
    NBAT = (KC + 3) // 4

    nc = bass.Bass("TRN2", target_bir_lowering=False)

    d_x = nc.dram_tensor("x", [T, D], F32, kind="ExternalInput").ap()
    d_wq = nc.dram_tensor("wq", [D, D], F32, kind="ExternalInput").ap()
    d_wkv = nc.dram_tensor("wkv", [D, KVW], F32, kind="ExternalInput").ap()
    d_bq = nc.dram_tensor("bq_r", [1, D], F32, kind="ExternalInput").ap()
    d_bkv = nc.dram_tensor("bkv_r", [1, KVW], F32, kind="ExternalInput").ap()
    d_gwt = nc.dram_tensor("gwt", [ICT, 128, KC * 128], I32, kind="ExternalInput").ap()
    d_uwt = nc.dram_tensor("uwt", [ICT, 128, KC * 128], I32, kind="ExternalInput").ap()
    d_dwt = nc.dram_tensor("dwt", [2, 2, ICH * 128, D2], I32, kind="ExternalInput").ap()
    d_gs = nc.dram_tensor("gs", [128, ICT], F32, kind="ExternalInput").ap()
    d_us = nc.dram_tensor("us", [128, ICT], F32, kind="ExternalInput").ap()
    d_dsr = nc.dram_tensor("ds_r", [1, D], F32, kind="ExternalInput").ap()
    d_idn = nc.dram_tensor("idn", [128, 128], F32, kind="ExternalInput").ap()
    d_xo = nc.dram_tensor("xo", [T, D], F32, kind="ExternalOutput").ap()
    d_gwb = nc.dram_tensor("gwb", [ICT, 128, KC * 128], BF16, kind="Internal").ap()
    d_uwb = nc.dram_tensor("uwb", [ICT, 128, KC * 128], BF16, kind="Internal").ap()
    d_dwb = nc.dram_tensor("dwb", [2, 2, ICH * 128, D2], BF16, kind="Internal").ap()

    stack = contextlib.ExitStack()
    b = Build(nc)
    b.stack = stack

    with stack:
        e = stack.enter_context
        ARB_FFN = ICH * TILE + 8 * D
        ARB_ATT = KC * D + KC * KVW + 2 * NSUB * KVW
        ARB_N = max(ARB_FFN, ARB_ATT)
        ARB = e(nc.sbuf_tensor("ARB", [128, ARB_N], BF16))
        XS0, XS1 = 0, D
        Y0, Y1 = 2 * D, 3 * D
        DSBC = 4 * D
        TMP = 5 * D
        NTMP = max(3 * TILE, D + KVW + KVW + 128)
        ARF_N = 5 * D + NTMP
        ARF = e(nc.sbuf_tensor("ARF", [128, ARF_N], F32))
        YT = e(nc.sbuf_tensor("YT", [128, KC * TILE], BF16))
        WG = e(nc.sbuf_tensor("WG", [128, 2 * KC * 128], BF16))
        WU = e(nc.sbuf_tensor("WU", [128, 2 * KC * 128], BF16))
        WD = e(nc.sbuf_tensor("WD", [128, 4 * D2], BF16))
        CON = e(nc.sbuf_tensor("CON", [128, 128 + 2 * ICT], F32))
        ST = e(nc.sbuf_tensor("ST", [128, 32], F32))
        PS = e(nc.psum_tensor("PS", [128, 4096], F32))

        arb, arf = ARB[:], ARF[:]
        arbf = ARB[:].bitcast(F32)
        arbi = ARB[:].bitcast(I32)
        arfi = ARF[:].bitcast(I32)
        yt, wg, wu, wd, con, st, ps = (X[:] for X in (YT, WG, WU, WD, CON, ST, PS))
        yt3 = yt.rearrange("p (k q) -> p k q", q=TILE)

        idn_ap = con[:, 0:128]

        def gs_ap(icg):
            return con[:, 128 + icg:129 + icg]

        def us_ap(icg):
            return con[:, 128 + ICT + icg:129 + ICT + icg]

        IGOFF = (ICH * TILE) // 2
        IUOFF = IGOFF + 2 * D
        WQB = 0
        WKVB = KC * D
        KVSF = (KC * D + KC * KVW) // 2

        def stx(ring, j):
            return st[:, ring * 16 + j:ring * 16 + j + 1]

        xo_flat = d_xo.rearrange("a b -> (a b)")
        x_flat = d_x.rearrange("a b -> (a b)")
        half_elems = T * D // 2
        for j in range(2):
            b.op("sync", lambda E, j=j: E.dma_start(
                out=xo_flat[j * half_elems:(j + 1) * half_elems],
                in_=x_flat[j * half_elems:(j + 1) * half_elems]),
                incs=[("xinit", 16)], dma=True)
        b.op("sync", lambda E: E.dma_start(out=con[:, 0:128], in_=d_idn),
             incs=[("cld", 16)], dma=True)
        b.op("sync", lambda E: E.dma_start(out=con[:, 128:128 + ICT], in_=d_gs),
             incs=[("cld", 16)], dma=True)
        b.op("sync", lambda E: E.dma_start(
            out=con[:, 128 + ICT:128 + 2 * ICT], in_=d_us), incs=[("cld", 16)], dma=True)

        def bcast_row(drow, width, ps_off, dst_off, scale):
            b.wait("sync", "pbc", b.n("pbc"))
            b.op("sync", lambda E: E.dma_start(
                out=arf[0:1, TMP:TMP + width], in_=drow), incs=[("cld", 16)], dma=True)
            b.wait("vector", "cld", b.n("cld"))
            b.op("vector", lambda E: E.memset(
                arf[0:1, TMP + width:TMP + width + 128], 1.0), incs=[("von", 1)])
            b.wait("tensor", "cld", b.n("cld"))
            b.wait("tensor", "von", b.n("von"))
            b.psum_fence("tensor")
            nmm = (width + 511) // 512
            for j in range(nmm):
                w = min(512, width - j * 512)
                b.op("tensor", lambda E, j=j, w=w: E.matmul(
                    ps[:, ps_off + j * 512:ps_off + j * 512 + w],
                    arf[0:1, TMP + width:TMP + width + 128],
                    arf[0:1, TMP + j * 512:TMP + j * 512 + w],
                    start=True, stop=True),
                    incs=([("pbc", 1)] if j == nmm - 1 else []))
            b.wait("vector", "pbc", b.n("pbc"))
            if scale is None:
                b.op("vector", lambda E: E.tensor_copy(
                    arf[:, dst_off:dst_off + width],
                    ps[:, ps_off:ps_off + width]), incs=[("vbc", 1)])
            else:
                b.op("vector", lambda E: E.tensor_scalar_mul(
                    arf[:, dst_off:dst_off + width],
                    ps[:, ps_off:ps_off + width], scale), incs=[("vbc", 1)])

        bcast_row(d_dsr, D, 0, DSBC, 0.5)
        b.wait("gpsimd", "xinit", 32)
        b.barrier()

        def ln_stage(tile, scalar_first_waits=()):
            base = {k: b.n(k) for k in ("sy", "vrs", "ptr", "vte", "ssq",
                                        "vvar", "sst", "vnb")}
            for eng, sname, val in scalar_first_waits:
                b.wait(eng, sname, val)
            b.psum_fence("tensor")
            for ts in range(NSUB):
                ring = ts % 2
                xs = arf[:, (XS0 if ring == 0 else XS1):][:, 0:D]
                yb = arf[:, (Y0 if ring == 0 else Y1):][:, 0:D]
                r0 = tile * TILE + ts * 128
                if ts >= 2:
                    b.wait("scalar", "sy", base["sy"] + ts - 1)
                    b.wait("scalar", "vrs", base["vrs"] + ts - 1)
                    b.wait("scalar", "ptr", base["ptr"] + NBAT * (ts - 1))
                b.op("scalar", lambda E, xs=xs, r0=r0: E.dma_start(
                    out=xs, in_=d_xo[r0:r0 + 128, :]),
                    incs=[(f"xld{ring}", 16)], dma=True)
                b.wait("scalar", f"xld{ring}", b.n(f"xld{ring}"))
                b.op("scalar", lambda E, xs=xs, yb=yb, ring=ring: E.activation(
                    out=yb, in_=xs, func=AF.Square, accum_out=stx(ring, 1)),
                    incs=[("ssq", 1)])
                b.wait("vector", f"xld{ring}", b.n(f"xld{ring}"))
                b.op("vector", lambda E, xs=xs, ring=ring: E.reduce_sum(
                    stx(ring, 0), xs, axis=AX.X), incs=[("vrs", 1)])
                b.wait("vector", "ssq", b.n("ssq"))
                b.op("vector", lambda E, ring=ring: E.tensor_scalar_mul(
                    stx(ring, 2), stx(ring, 0), 1.0 / D))
                b.op("vector", lambda E, ring=ring: E.tensor_mul(
                    stx(ring, 3), stx(ring, 2), stx(ring, 2)))
                b.op("vector", lambda E, ring=ring: E.scalar_tensor_tensor(
                    out=stx(ring, 4), in0=stx(ring, 1), scalar=1.0 / D,
                    in1=stx(ring, 3), op0=ALU.mult, op1=ALU.subtract))
                b.op("vector", lambda E, ring=ring: E.tensor_scalar_add(
                    stx(ring, 4), stx(ring, 4), EPS), incs=[("vvar", 1)])
                b.wait("scalar", "vvar", b.n("vvar"))
                b.op("scalar", lambda E, ring=ring: E.activation(
                    out=stx(ring, 5), in_=stx(ring, 4), func=AF.Sqrt),
                    incs=[("sst", 1)])
                b.wait("vector", "sst", b.n("sst"))
                b.op("vector", lambda E, ring=ring: E.reciprocal(
                    stx(ring, 6), stx(ring, 5)))
                b.op("vector", lambda E, ring=ring: E.tensor_scalar(
                    out=stx(ring, 7), in0=stx(ring, 2), scalar1=stx(ring, 6),
                    scalar2=-1.0, op0=ALU.mult, op1=ALU.mult),
                    incs=[("vnb", 1)])
                b.wait("scalar", "vnb", b.n("vnb"))
                b.op("scalar", lambda E, xs=xs, yb=yb, ring=ring: E.activation(
                    out=yb, in_=xs, func=AF.Identity, bias=stx(ring, 7),
                    scale=stx(ring, 6)), incs=[("sy", 1)])
                b.wait("tensor", "sy", b.n("sy"))
                for bt in range(NBAT):
                    js = list(range(bt * 4, min(KC, bt * 4 + 4)))
                    bank = (ts * NBAT + bt) % 4
                    nb_done = b.n("vte")
                    if nb_done - base["vte"] >= 4:
                        b.wait("tensor", "vte", nb_done - 3)
                    for jj, j in enumerate(js):
                        b.op("tensor", lambda E, j=j, jj=jj, bank=bank, yb=yb:
                             E.transpose(
                                 ps[:, bank * 512 + jj * 128:
                                    bank * 512 + (jj + 1) * 128],
                                 yb[:, j * 128:(j + 1) * 128], idn_ap),
                             incs=([("ptr", 1)] if j == js[-1] else []))
                    b.wait("vector", "ptr", b.n("ptr"))
                    nj = len(js)
                    src = ps[:, bank * 512:bank * 512 + nj * 128].rearrange(
                        "p (j q) -> p j q", q=128)
                    dst = yt3[:, bt * 4:bt * 4 + nj, ts * 128:ts * 128 + 128]
                    b.op("vector", lambda E, src=src, dst=dst: E.tensor_copy(
                        dst, src), incs=[("vte", 1)])
            return b.n("vte")

        def ffn_phase(tile, is_int):
            sfw = []
            yT_ready = ln_stage(tile, scalar_first_waits=sfw)
            ln_ptr_final = b.n("ptr")

            for half in range(2):
                hh_base = b.n("vhh")
                flush_base_prev_half = b.n("vfl")
                for ic in range(ICH):
                    icg = half * ICH + ic
                    par = ic % 2
                    pg = ps[:, par * 2048:par * 2048 + TILE]
                    pu = ps[:, par * 2048 + 1024:par * 2048 + 1024 + TILE]
                    for stream in ("g", "u"):
                        dint = d_gwt if stream == "g" else d_uwt
                        dscr = d_gwb if stream == "g" else d_uwb
                        wring = wg if stream == "g" else wu
                        nslab = b.n(stream + "slab")
                        b.cnt[stream + "slab"] = nslab + 1
                        slot = nslab % 2
                        soff = slot * KC * 128
                        pe_done = (lambda n: 2 * n + 1) if stream == "g" else \
                            (lambda n: 2 * n + 2)
                        wbrec = b.aux.setdefault(stream + "wbrec", {})
                        if is_int:
                            ioff = (IGOFF if stream == "g" else IUOFF) + slot * D
                            if nslab >= 2:
                                b.wait("sync", stream + "dq", nslab - 1)
                            b.op("sync", lambda E, dint=dint, icg=icg, ioff=ioff:
                                 E.dma_start(out=arbi[:, ioff:ioff + D],
                                             in_=dint[icg]),
                                 incs=[(f"{stream}il{slot}", 16)], dma=True)
                            if nslab >= 2:
                                b.wait("scalar", "pgu", pe_done(nslab - 2))
                                wname, wval = wbrec[nslab - 2]
                                b.wait("scalar", wname, wval)
                            b.wait("scalar", f"{stream}il{slot}",
                                   b.n(f"{stream}il{slot}"))
                            b.op("scalar", lambda E, ioff=ioff, wring=wring,
                                 soff=soff: E.activation(
                                     out=wring[:, soff:soff + KC * 128],
                                     in_=arbi[:, ioff:ioff + D],
                                     func=AF.Copy, bias=-128.0),
                                 incs=[(stream + "dq", 1)])
                            b.wait("gpsimd", stream + "dq", nslab + 1)
                            b.op("gpsimd", lambda E, dscr=dscr, icg=icg,
                                 wring=wring, soff=soff: E.dma_start(
                                     out=dscr[icg],
                                     in_=wring[:, soff:soff + KC * 128]),
                                 incs=[(f"{stream}wb{slot}", 16)], dma=True)
                            wbrec[nslab] = (f"{stream}wb{slot}",
                                            b.n(f"{stream}wb{slot}"))
                            b.wait("tensor", stream + "dq", nslab + 1)
                        else:
                            if nslab >= 2:
                                b.wait("sync", "pgu", pe_done(nslab - 2))
                            b.op("sync", lambda E, dscr=dscr, icg=icg,
                                 wring=wring, soff=soff: E.dma_start(
                                     out=wring[:, soff:soff + KC * 128],
                                     in_=dscr[icg]),
                                 incs=[(f"{stream}ld{slot}", 16)], dma=True)
                            b.wait("tensor", f"{stream}ld{slot}",
                                   b.n(f"{stream}ld{slot}"))
                        dst = pg if stream == "g" else pu
                        if stream == "g":
                            if icg == 0:
                                b.wait("tensor", "vte", yT_ready)
                            if ic < 2:
                                b.wait("tensor", "vfl", flush_base_prev_half)
                            else:
                                b.wait("tensor", "vhh", hh_base + ic - 1)
                        for k in range(KC):
                            for j in range(NMMT):
                                w = min(512, TILE - j * 512)
                                last = (k == KC - 1 and j == NMMT - 1)
                                b.op("tensor", lambda E, k=k, j=j, w=w, dst=dst,
                                     wring=wring, soff=soff: E.matmul(
                                         dst[:, j * 512:j * 512 + w],
                                         wring[:, soff + k * 128:
                                               soff + (k + 1) * 128],
                                         yt[:, k * TILE + j * 512:
                                            k * TILE + j * 512 + w],
                                         start=(k == 0), stop=(k == KC - 1)),
                                     incs=([("pgu", 1)] if last else []))
                    sg = arf[:, (Y0 if par == 0 else Y1):][:, 0:TILE]
                    if ic < 2:
                        b.wait("scalar", "ptr", ln_ptr_final)
                    else:
                        b.wait("scalar", "vhh", hh_base + ic - 1)
                    b.wait("scalar", "pgu", b.n("pgu") - 1)
                    b.op("scalar", lambda E, pg=pg, sg=sg, icg=icg: E.activation(
                        out=sg, in_=pg, func=AF.Silu, scale=gs_ap(icg)),
                        incs=[("ssi", 1)])
                    b.wait("vector", "pgu", b.n("pgu"))
                    b.wait("vector", "ssi", b.n("ssi"))
                    b.wait("vector", "dpe", b.aux.get("half_dpe", 0))
                    b.op("vector", lambda E, sg=sg, pu=pu, ic=ic, icg=icg:
                         E.scalar_tensor_tensor(
                             out=arb[:, ic * TILE:(ic + 1) * TILE],
                             in0=sg, scalar=us_ap(icg), in1=pu,
                             op0=ALU.mult, op1=ALU.mult),
                         incs=[("vhh", 1)])
                hh_done = b.n("vhh")

                ntq = (NSUB + 3) // 4
                nch_per = max(1, ICH // 2)
                if is_int:
                    b.wait("sync", "vhh", hh_done)
                first_group = True
                for tq in range(ntq):
                    tss = list(range(tq * 4, min(NSUB, tq * 4 + 4)))
                    for hh2 in range(2):
                        grp_flush_base = b.n("vfl")
                        grp_pdn_base = b.n("pdn")
                        for ch in range(nch_per):
                            ics = ([2 * ch, 2 * ch + 1] if ICH >= 2 else [0])
                            nsl = len(ics)
                            nch = b.n("dch")
                            b.cnt["dch"] = nch + 1
                            slot = nch % 2
                            soff = slot * 2 * D2
                            src_i = d_dwt[half, hh2].rearrange(
                                "(cc p) h -> p cc h", p=128)[
                                :, ics[0]:ics[0] + nsl, :]
                            src_b = d_dwb[half, hh2].rearrange(
                                "(cc p) h -> p cc h", p=128)[
                                :, ics[0]:ics[0] + nsl, :]
                            dstw = wd[:, soff:soff + nsl * D2].rearrange(
                                "p (cc h) -> p cc h", h=D2)
                            chunkwb = b.aux.setdefault("dch_wb", {})
                            if is_int and tq == 0:
                                idch = b.n("idch")
                                b.cnt["idch"] = idch + 1
                                islot = idch % 2
                                ioff = Y0 + islot * D
                                if idch >= 2:
                                    b.wait("sync", "ddq", idch - 1)
                                b.op("sync", lambda E, src_i=src_i, ioff=ioff,
                                     nsl=nsl: E.dma_start(
                                         out=arfi[:, ioff:ioff + nsl * D2]
                                         .rearrange("p (cc h) -> p cc h", h=D2),
                                         in_=src_i),
                                     incs=[(f"dil{islot}", 16)], dma=True)
                                if nch >= 2:
                                    b.wait("scalar", "dpe", nch - 1)
                                    wbprev = chunkwb.get(nch - 2)
                                    if wbprev is not None:
                                        b.wait("scalar", wbprev[0], wbprev[1])
                                b.wait("scalar", f"dil{islot}",
                                       b.n(f"dil{islot}"))
                                b.op("scalar", lambda E, ioff=ioff, soff=soff,
                                     nsl=nsl: E.activation(
                                         out=wd[:, soff:soff + nsl * D2],
                                         in_=arfi[:, ioff:ioff + nsl * D2],
                                         func=AF.Copy, bias=-128.0),
                                     incs=[("ddq", 1)])
                                b.wait("gpsimd", "ddq", b.n("ddq"))
                                b.op("gpsimd", lambda E, src_b=src_b, dstw=dstw:
                                     E.dma_start(out=src_b, in_=dstw),
                                     incs=[(f"dwb{slot}", 16)], dma=True)
                                chunkwb[nch] = (f"dwb{slot}", b.n(f"dwb{slot}"))
                                b.wait("tensor", "ddq", b.n("ddq"))
                            else:
                                if is_int and tq == 1 and hh2 == 0 and ch == 0:
                                    b.wait("sync", "dwb0", b.n("dwb0"))
                                    b.wait("sync", "dwb1", b.n("dwb1"))
                                if nch >= 2:
                                    b.wait("sync", "dpe", nch - 1)
                                    wbprev = chunkwb.get(nch - 2)
                                    if wbprev is not None:
                                        b.wait("sync", wbprev[0], wbprev[1])
                                chunkwb[nch] = None
                                b.op("sync", lambda E, src_b=src_b, dstw=dstw:
                                     E.dma_start(out=dstw, in_=src_b),
                                     incs=[(f"dld{slot}", 16)], dma=True)
                                b.wait("tensor", f"dld{slot}",
                                       b.n(f"dld{slot}"))
                            if first_group and ch == 0:
                                b.wait("tensor", "vhh", hh_done)
                                first_group = False
                            for jj, ic in enumerate(ics):
                                for tsi, tsv in enumerate(tss):
                                    psd = ps[:, tsi * 1024:tsi * 1024 + D2]
                                    if ic == 0:
                                        want = grp_flush_base - len(tss) + tsi + 1
                                        b.wait("tensor", "vfl", want)
                                    for j in range(ND2):
                                        w = min(512, D2 - j * 512)
                                        lastmm = (ic == ICH - 1 and j == ND2 - 1)
                                        lastofchunk = (jj == nsl - 1 and
                                                       tsi == len(tss) - 1 and
                                                       j == ND2 - 1)
                                        incs = []
                                        if lastofchunk:
                                            incs.append(("dpe", 1))
                                        elif lastmm and tsi < len(tss) - 1:
                                            incs.append(("pdn", 1))
                                        b.op("tensor", lambda E, j=j, w=w,
                                             psd=psd, soff=soff, jj=jj,
                                             tsv=tsv, ic=ic: E.matmul(
                                                 psd[:, j * 512:j * 512 + w],
                                                 arb[:, ic * TILE + tsv * 128:
                                                     ic * TILE + tsv * 128 + 128],
                                                 wd[:, soff + jj * D2 + j * 512:
                                                    soff + jj * D2 + j * 512 + w],
                                                 start=(ic == 0),
                                                 stop=(ic == ICH - 1)),
                                             incs=incs)
                        for tsi, tsv in enumerate(tss):
                            psd = ps[:, tsi * 1024:tsi * 1024 + D2]
                            nfl = b.n("vfl")
                            tslot = nfl % 3
                            toff = TMP + tslot * TILE
                            if tsi < len(tss) - 1:
                                b.wait("vector", "pdn", grp_pdn_base + tsi + 1)
                            else:
                                b.wait("vector", "dpe", b.n("dch"))
                            b.wait("vector", f"gac{tslot}",
                                   b.n(f"gac{tslot}"))
                            b.op("vector", lambda E, psd=psd, toff=toff,
                                 hh2=hh2: E.tensor_tensor(
                                     out=arf[:, toff:toff + D2], in0=psd,
                                     in1=arf[:, DSBC + hh2 * D2:
                                             DSBC + hh2 * D2 + D2],
                                     op=ALU.mult),
                                 incs=[("vfl", 1)])
                            r0 = tile * TILE + tsv * 128
                            b.wait("gpsimd", "vfl", b.n("vfl"))
                            accrec = b.aux.setdefault("accrec", {})
                            prevacc = accrec.get((r0, hh2))
                            if prevacc is not None:
                                b.wait("gpsimd", prevacc[0], prevacc[1])
                            b.op("gpsimd", lambda E, toff=toff, r0=r0, hh2=hh2:
                                 E.dma_start(
                                     out=d_xo[r0:r0 + 128,
                                              hh2 * D2:hh2 * D2 + D2],
                                     in_=arf[:, toff:toff + D2],
                                     accum_op=ALU.add),
                                 incs=[(f"gac{tslot}", 16)], dma=True)
                            accrec[(r0, hh2)] = (f"gac{tslot}",
                                                 b.n(f"gac{tslot}"))
                b.aux["half_dpe"] = b.n("dch")
            for sl in range(3):
                b.wait("gpsimd", f"gac{sl}", b.n(f"gac{sl}"))
            b.barrier()

        def attn_phase(tile, first):
            BKVB = TMP + D + KVW
            if first:
                for k in range(KC):
                    xsoff = XS0 if (k % 2) == 0 else XS1
                    if k >= 2:
                        b.wait("sync", "vqc", b.n("vqc") - 1)
                    b.op("sync", lambda E, k=k, xsoff=xsoff: E.dma_start(
                        out=arf[:, xsoff:xsoff + D],
                        in_=d_wq[k * 128:(k + 1) * 128, :]),
                        incs=[(f"qld{k % 2}", 16)], dma=True)
                    b.wait("vector", f"qld{k % 2}", b.n(f"qld{k % 2}"))
                    b.op("vector", lambda E, k=k, xsoff=xsoff: E.tensor_copy(
                        arb[:, WQB + k * D:WQB + (k + 1) * D],
                        arf[:, xsoff:xsoff + D]), incs=[("vqc", 1)])
                for k in range(KC):
                    b.wait("sync", "vqc", b.n("vqc"))
                    b.op("sync", lambda E, k=k: E.dma_start(
                        out=arf[:, TMP:TMP + KVW],
                        in_=d_wkv[k * 128:(k + 1) * 128, :]),
                        incs=[(f"qld{k % 2}", 16)], dma=True)
                    b.wait("vector", f"qld{k % 2}", b.n(f"qld{k % 2}"))
                    b.op("vector", lambda E, k=k: E.tensor_copy(
                        arb[:, WKVB + k * KVW:WKVB + (k + 1) * KVW],
                        arf[:, TMP:TMP + KVW]), incs=[("vqc", 1)])
                b.wait("sync", "vqc", b.n("vqc"))
                bcast_row(d_bq, D, 0, DSBC, None)
                bcast_row(d_bkv, KVW, 2048, BKVB, None)
            yT_ready = ln_stage(tile, scalar_first_waits=(
                [("scalar", "vqc", b.n("vqc"))] if first else []))

            kv_base = b.n("vkv")
            for ts in range(NSUB):
                par = ts % 2
                pkv = ps[:, par * 1024:par * 1024 + KVW]
                if ts == 0:
                    b.wait("tensor", "vte", yT_ready)
                if ts >= 2:
                    b.wait("tensor", "vkv", kv_base + ts - 1)
                for k in range(KC):
                    for j in range(NKV):
                        w = min(512, KVW - j * 512)
                        last = (k == KC - 1 and j == NKV - 1)
                        b.op("tensor", lambda E, k=k, j=j, w=w, pkv=pkv, ts=ts:
                             E.matmul(
                                 pkv[:, j * 512:j * 512 + w],
                                 yt3[:, k, ts * 128:ts * 128 + 128],
                                 arb[:, WKVB + k * KVW + j * 512:
                                     WKVB + k * KVW + j * 512 + w],
                                 start=(k == 0), stop=(k == KC - 1)),
                             incs=([("pkv", 1)] if last else []))
                b.wait("vector", "pkv", b.n("pkv"))
                b.op("vector", lambda E, pkv=pkv, ts=ts: E.tensor_add(
                    arbf[:, KVSF + ts * KVW:KVSF + (ts + 1) * KVW], pkv,
                    arf[:, BKVB:BKVB + KVW]), incs=[("vkv", 1)])

            qb_base = b.n("vqb")
            for ts in range(NSUB):
                par = ts % 2
                pq = ps[:, par * 2048:par * 2048 + D]
                if ts < 2:
                    b.wait("tensor", "vkv", b.n("vkv"))
                else:
                    b.wait("tensor", "vqb", qb_base + ts - 1)
                for k in range(KC):
                    for j in range(NQ):
                        w = min(512, D - j * 512)
                        last = (k == KC - 1 and j == NQ - 1)
                        b.op("tensor", lambda E, k=k, j=j, w=w, pq=pq, ts=ts:
                             E.matmul(
                                 pq[:, j * 512:j * 512 + w],
                                 yt3[:, k, ts * 128:ts * 128 + 128],
                                 arb[:, WQB + k * D + j * 512:
                                     WQB + k * D + j * 512 + w],
                                 start=(k == 0), stop=(k == KC - 1)),
                             incs=([("pq", 1)] if last else []))
                b.wait("vector", "pq", b.n("pq"))
                b.op("vector", lambda E, pq=pq: E.tensor_add(
                    arf[:, XS0:XS0 + D], pq, arf[:, DSBC:DSBC + D]),
                    incs=[("vqb", 1)])
                kv0 = KVSF + ts * KVW
                kview = arbf[:, kv0:kv0 + KVW].rearrange(
                    "p (g c d) -> p g c d", g=G, c=2)[:, :, 0, :] \
                    .unsqueeze(2).broadcast_to([128, G, HPG, HD])
                qview = arf[:, XS0:XS0 + D].rearrange(
                    "p (g h d) -> p g h d", g=G, h=HPG)
                b.op("vector", lambda E, kview=kview, qview=qview:
                     E.tensor_tensor(
                         out=arf[:, XS1:XS1 + D].rearrange(
                             "p (g h d) -> p g h d", g=G, h=HPG),
                         in0=qview, in1=kview, op=ALU.mult))
                b.op("vector", lambda E, ts=ts: E.reduce_sum(
                    arf[:, TMP + ts * GH:TMP + (ts + 1) * GH],
                    arf[:, XS1:XS1 + D].rearrange("p (s d) -> p s d", d=HD),
                    axis=AX.X))
            SALL = TMP
            S8 = TMP + NSUB * GH
            D0 = S8 + NSUB * GH
            EE = D0 + NSUB * GH
            ATT = EE + NSUB * GH
            nsc = NSUB * GH
            b.op("vector", lambda E: E.tensor_scalar_mul(
                arf[:, S8:S8 + nsc], arf[:, SALL:SALL + nsc], 1.0 / 8.0))
            b.op("vector", lambda E: E.tensor_sub(
                arf[:, D0:D0 + nsc], arf[:, S8:S8 + nsc],
                arf[:, S8:S8 + nsc]), incs=[("vd0", 1)])
            b.wait("scalar", "vd0", b.n("vd0"))
            b.op("scalar", lambda E: E.activation(
                out=arf[:, EE:EE + nsc], in_=arf[:, D0:D0 + nsc], func=AF.Exp),
                incs=[("sxp", 1)])
            b.wait("vector", "sxp", b.n("sxp"))
            b.op("vector", lambda E: E.reciprocal(
                arf[:, D0:D0 + nsc], arf[:, EE:EE + nsc]))
            b.op("vector", lambda E: E.tensor_mul(
                arf[:, ATT:ATT + nsc], arf[:, EE:EE + nsc],
                arf[:, D0:D0 + nsc]))
            for ts in range(NSUB):
                oslot = XS1 if ts % 2 else XS0
                b.wait("vector", f"aac{ts % 2}", b.n(f"aac{ts % 2}"))
                kv0 = KVSF + ts * KVW
                aview = arf[:, ATT + ts * GH:ATT + (ts + 1) * GH].rearrange(
                    "p (g h) -> p g h", g=G).unsqueeze(3) \
                    .broadcast_to([128, G, HPG, HD])
                vview = arbf[:, kv0:kv0 + KVW].rearrange(
                    "p (g c d) -> p g c d", g=G, c=2)[:, :, 1, :] \
                    .unsqueeze(2).broadcast_to([128, G, HPG, HD])
                b.op("vector", lambda E, aview=aview, vview=vview, oslot=oslot:
                     E.tensor_tensor(
                         out=arf[:, oslot:oslot + D].rearrange(
                             "p (g h d) -> p g h d", g=G, h=HPG),
                         in0=aview, in1=vview, op=ALU.mult),
                     incs=[("voa", 1)])
                r0 = tile * TILE + ts * 128
                b.wait("gpsimd", "voa", b.n("voa"))
                accrec = b.aux.setdefault("accrec", {})
                for hh2 in range(2):
                    prevacc = accrec.get((r0, hh2))
                    if prevacc is not None:
                        b.wait("gpsimd", prevacc[0], prevacc[1])
                b.op("gpsimd", lambda E, oslot=oslot, r0=r0, ts=ts:
                     E.dma_start(
                         out=d_xo[r0:r0 + 128, :], in_=arf[:, oslot:oslot + D],
                         accum_op=ALU.add), incs=[(f"aac{ts % 2}", 16)],
                     dma=True)
                for hh2 in range(2):
                    accrec[(r0, hh2)] = (f"aac{ts % 2}", b.n(f"aac{ts % 2}"))
            for sl in range(2):
                b.wait("gpsimd", f"aac{sl}", b.n(f"aac{sl}"))
            b.barrier()

        b.wait("scalar", "xinit", 32)
        b.wait("scalar", "cld", b.n("cld"))
        b.wait("vector", "vbc", b.n("vbc"))
        ffn_phase(0, is_int=True)
        for s in ("gwb0", "gwb1", "uwb0", "uwb1", "dwb0", "dwb1"):
            b.wait("sync", s, b.n(s))
        ffn_phase(1, is_int=False)
        attn_phase(0, first=True)
        attn_phase(1, first=False)
        b.wait("sync", "voa", b.n("voa"))
        bcast_row(d_dsr, D, 0, DSBC, 0.5)
        b.barrier()
        ffn_phase(0, is_int=False)
        ffn_phase(1, is_int=False)
        for sl in range(3):
            b.wait("gpsimd", f"gac{sl}", b.n(f"gac{sl}"))

        with nc.Block() as block:
            for eng in ENGS:
                def body(E, eng=eng):
                    for fn in b.ops[eng]:
                        fn(E)
                getattr(block, eng)(body)

    return nc



def kernel(L0_attn_x, L0_attn_Wq, L0_attn_bq, L0_attn_Wkv, L0_attn_bkv,
           L0_ffn_gate_w, L0_ffn_gate_scale, L0_ffn_up_w, L0_ffn_up_scale,
           L0_ffn_down_w, L0_ffn_down_scale):
    cfg = real_cfg()
    c = derive(cfg)
    n_cores = 8
    in_maps = host_prep(cfg, L0_attn_x, L0_attn_Wq, L0_attn_bq, L0_attn_Wkv,
                        L0_attn_bkv, L0_ffn_gate_w, L0_ffn_gate_scale,
                        L0_ffn_up_w, L0_ffn_up_scale, L0_ffn_down_w,
                        L0_ffn_down_scale, n_cores)
    nc = build_program(cfg)
    res = run_bass_kernel_spmd(nc, in_maps, core_ids=list(range(n_cores)))
    out = np.concatenate([np.asarray(r["xo"], dtype=np.float32)
                          for r in res.results], axis=0)
    return out.reshape(1, c["T"] * n_cores, c["D"])
